# version 02
# baseline (speedup 1.0000x reference)
import sys
if '/opt/trn_rl_repo' not in sys.path:
    sys.path.insert(0, '/opt/trn_rl_repo')

from contextlib import ExitStack

import numpy as np

import concourse.bass as bass
import concourse.mybir as mybir
from concourse.bass_utils import run_bass_kernel_spmd

F32 = mybir.dt.float32
F16 = mybir.dt.float16
BF16 = mybir.dt.bfloat16
I32 = mybir.dt.int32
AF = mybir.ActivationFunctionType
ALU = mybir.AluOpType

NCORES = 8
TCORE = 2048
NCH = 4
TCH = 512
NTS = 4
D = 4096
FT = 32
NH = 8
HE = 1024
INTER = 11008
NTI = 86
NOC = 8
GU_PROWS = 512
D_PROWS = 1376
EPS_LN = 1e-5
EPS_DEN = 1e-6

GU_BLOCKS = [8] * 10 + [6]
GU_SLICES = [2048] * 5 + [768]
D_OSL = 2048
N_DSL = 2


class Emitter:
    ENGINES = ("sync", "scalar", "vector", "tensor", "gpsimd")

    def __init__(self, nc, max_ops=150):
        self.nc = nc
        self.max_ops = max_ops
        self.bufs = {e: [] for e in self.ENGINES}

    def add(self, eng, fn):
        self.bufs[eng].append(fn)
        if len(self.bufs[eng]) >= self.max_ops:
            self.flush()

    def flush(self):
        if not any(self.bufs.values()):
            return
        bufs = self.bufs
        self.bufs = {e: [] for e in self.ENGINES}
        with self.nc.Block() as block:
            if bufs["sync"]:
                @block.sync
                def _(eng, ops=bufs["sync"]):
                    for f in ops:
                        f(eng)
            if bufs["scalar"]:
                @block.scalar
                def _(eng, ops=bufs["scalar"]):
                    for f in ops:
                        f(eng)
            if bufs["vector"]:
                @block.vector
                def _(eng, ops=bufs["vector"]):
                    for f in ops:
                        f(eng)
            if bufs["tensor"]:
                @block.tensor
                def _(eng, ops=bufs["tensor"]):
                    for f in ops:
                        f(eng)
            if bufs["gpsimd"]:
                @block.gpsimd
                def _(eng, ops=bufs["gpsimd"]):
                    for f in ops:
                        f(eng)


class Sem:
    def __init__(self, handle, name):
        self.h = handle
        self.name = name
        self.val = 0

    def inc(self, by=1):
        self.val += by
        return self.val


def build_program(dbg=None, stages=("setup", "zps", "dq", "attn", "ffn"), nch=NCH):
    nc = bass.Bass("TRN2", target_bir_lowering=False)

    x_in = nc.dram_tensor("x", [TCORE, D], F32, kind="ExternalInput").ap()
    wq_in = nc.dram_tensor("wq", [HE, D], F32, kind="ExternalInput").ap()
    bq_in = nc.dram_tensor("bq", [NH, 128], F32, kind="ExternalInput").ap()
    wo_in = nc.dram_tensor("wo", [D, D], F32, kind="ExternalInput").ap()
    m_in = nc.dram_tensor("m", [NH, 128, 128], F32, kind="ExternalInput").ap()
    z_in = nc.dram_tensor("z", [NH, 128], F32, kind="ExternalInput").ap()
    gq_in = nc.dram_tensor("gq", [GU_PROWS, INTER], I32, kind="ExternalInput").ap()
    gs_in = nc.dram_tensor("gs", [32, INTER], F16, kind="ExternalInput").ap()
    gz_in = nc.dram_tensor("gz", [32, INTER // 8], I32, kind="ExternalInput").ap()
    uq_in = nc.dram_tensor("uq", [GU_PROWS, INTER], I32, kind="ExternalInput").ap()
    us_in = nc.dram_tensor("us", [32, INTER], F16, kind="ExternalInput").ap()
    uz_in = nc.dram_tensor("uz", [32, INTER // 8], I32, kind="ExternalInput").ap()
    dq_in = nc.dram_tensor("dq", [D_PROWS, D], I32, kind="ExternalInput").ap()
    ds_in = nc.dram_tensor("ds", [NTI, D], F16, kind="ExternalInput").ap()
    dz_in = nc.dram_tensor("dz", [NTI, D // 8], I32, kind="ExternalInput").ap()
    ident_in = nc.dram_tensor("ident", [128, 128], F32, kind="ExternalInput").ap()
    selones_in = nc.dram_tensor("selones", [NH, NH * 128], F32, kind="ExternalInput").ap()
    shift_in = nc.dram_tensor("shiftv", [128, 1], I32, kind="ExternalInput").ap()
    y_out = nc.dram_tensor("y", [TCORE, D], F32, kind="ExternalOutput").ap()

    wqt_scr = nc.dram_tensor("wqt_scr", [FT, 128, HE], BF16, kind="Internal").ap()
    wot_scr = nc.dram_tensor("wot_scr", [NH, 128, D], BF16, kind="Internal").ap()
    g_scr = nc.dram_tensor("g_scr", [FT, 128, INTER], BF16, kind="Internal").ap()
    u_scr = nc.dram_tensor("u_scr", [FT, 128, INTER], BF16, kind="Internal").ap()
    d_scr = nc.dram_tensor("d_scr", [NOC, NTI, 128, 512], BF16, kind="Internal").ap()
    zpg_scr = nc.dram_tensor("zpg_scr", [32, INTER], F16, kind="Internal").ap()
    zpu_scr = nc.dram_tensor("zpu_scr", [32, INTER], F16, kind="Internal").ap()
    zpd_scr = nc.dram_tensor("zpd_scr", [NTI, D], F16, kind="Internal").ap()
    h2_scr = nc.dram_tensor("h2_scr", [NCH, TCH, D], BF16, kind="Internal").ap()
    x2_scr = nc.dram_tensor("x2_scr", [NCH, TCH, D], F32, kind="Internal").ap()

    dbg_outs = {}
    if dbg:
        for name, shape, dt, key in dbg:
            dbg_outs[name] = (nc.dram_tensor("dbg_" + name, shape, dt, kind="ExternalOutput").ap(), key)

    with ExitStack() as ctx:
        sb = lambda name, shape, dt: ctx.enter_context(nc.sbuf_tensor(name, shape, dt))

        A1 = sb("A1", [128, 45056], BF16)
        A2 = sb("A2", [128, 16384], BF16)
        A3 = sb("A3", [128, 18432], BF16)
        A4 = sb("A4", [128, 20480], BF16)

        def carve(arena, off, shape, dt):
            scale = 2 if dt in (F32, I32) else 1
            assert off % scale == 0
            v = arena.ap() if dt == BF16 else arena.ap().bitcast(dt)
            start = off // scale
            n_bf16 = scale * shape[1]
            return v[0:shape[0], start:start + shape[1]], off + n_bf16

        off = 0
        h1b = []
        for i in range(2):
            v, off = carve(A1, off, [128, 1024], F32)
            h1b.append(v)
        sigT, off = carve(A1, off, [128, NH * 512], BF16)
        erex = []
        for i in range(6):
            v, off = carve(A1, off, [128, 512], F32)
            erex.append(v)
        rdb = []
        for i in range(2):
            v, off = carve(A1, off, [128, 512], F32)
            rdb.append(v)
        den_e, off = carve(A1, off, [8, 512], F32)
        rden, off = carve(A1, off, [8, 512], F32)
        retT, off = carve(A1, off, [128, NH * 512], BF16)
        x2t = []
        for i in range(2):
            v, off = carve(A1, off, [128, D], F32)
            x2t.append(v)
        h2b = []
        for i in range(2):
            v, off = carve(A1, off, [128, 1024], BF16)
            h2b.append(v)
        assert off <= 45056, off
        zoff = 0
        zsb, zoff = carve(A1, zoff, [NTI, INTER // 8], I32)
        zpi, zoff = carve(A1, zoff, [NTI, 5504], I32)
        szs16, zoff = carve(A1, zoff, [NTI, 5504], F16)
        zps16, zoff = carve(A1, zoff, [NTI, 5504], F16)
        assert zoff <= 45056, zoff
        inter = [A1.ap()[:, 512 * nt: 512 * (nt + 1)] for nt in range(NTI)]

        off = 0
        xt = []
        for i in range(2):
            v, off = carve(A2, off, [128, D], F32)
            xt.append(v)
        assert off <= 16384
        h2T = [A2.ap()[:, 512 * ft: 512 * (ft + 1)] for ft in range(FT)]

        off = 0
        h1T_all, off = carve(A3, off, [128, FT * 512], BF16)
        h1T = [h1T_all[:, 512 * ft: 512 * (ft + 1)] for ft in range(FT)]
        wqtb = []
        for i in range(2):
            v, off = carve(A3, off, [128, HE], BF16)
            wqtb.append(v)
        assert off <= 18432, off
        off = 0
        wotb = []
        for i in range(2):
            v, off = carve(A3, off, [128, D], BF16)
            wotb.append(v)
        off = 0
        silu_st = []
        for i in range(8):
            v, off = carve(A3, off, [128, 512], F32)
            silu_st.append(v)
        gub = []
        for i in range(3):
            v, off = carve(A3, off, [128, 1024], BF16)
            gub.append(v)
        wdb = []
        for i in range(2):
            v, off = carve(A3, off, [128, 512], BF16)
            wdb.append(v)
        f16b = []
        for i in range(2):
            v, off = carve(A3, off, [128, 512], F16)
            f16b.append(v)
        x2cb = []
        for i in range(2):
            v, off = carve(A3, off, [128, 512], F32)
            x2cb.append(v)
        outcb = []
        for i in range(2):
            v, off = carve(A3, off, [128, 512], F32)
            outcb.append(v)
        assert off <= 18432, off

        off = 0
        qrep, off = carve(A4, off, [128, 2048], I32)
        nib, off = carve(A4, off, [128, 2048], I32)
        sbb = []
        for i in range(2):
            v, off = carve(A4, off, [128, 2048], F16)
            sbb.append(v)
        zpbb = []
        for i in range(2):
            v, off = carve(A4, off, [128, 2048], F16)
            zpbb.append(v)
        t16, off = carve(A4, off, [128, 2048], F16)
        wdq, off = carve(A4, off, [128, 2048], BF16)
        assert off <= 20480, off

        ident = sb("sb_ident", [128, 128], F32)
        selones = sb("sb_selones", [NH, NH * 128], F32)
        shiftv = sb("sb_shiftv", [128, 1], I32)
        m_bf = sb("m_bf", [128, NH * 128], BF16)
        m_f32 = sb("m_f32", [128, 128], F32)
        zbd = sb("zbd", [128, NH * NH], BF16)
        zstage = sb("zstage", [NH, 128], F32)
        ztb = sb("ztb", [128, NH], BF16)
        bqstage = sb("bqstage", [NH, 128], F32)
        bqs = sb("bqs", [128, NH], F32)
        c_eps = sb("c_eps", [128, 1], F32)
        c_nh = sb("c_nh", [128, 1], F32)
        stats = sb("stats", [128, 48], F32)
        mv = sb("mv", [128, 2], F32)
        lnv = sb("lnv", [128, 1], F32)
        rstd = sb("rstd", [128, 1], F32)
        mb = sb("mb", [128, 1], F32)
        evst = [sb(f"evst{i}", [128, 128], BF16) for i in range(2)]

        pb = [ctx.enter_context(nc.psum_tensor(f"pb{i}", [128, 512], F32)) for i in range(8)]

        def mksem(name):
            return Sem(ctx.enter_context(nc.semaphore(name)), name)

        s_wq = mksem("s_wq")
        s_ld = mksem("s_ld")
        s_sp = mksem("s_sp")
        s_rep = mksem("s_rep")
        s_bc = mksem("s_bc")
        s_dqw = mksem("s_dqw")
        s_out = mksem("s_out")
        s_pe = mksem("s_pe")
        s_act = mksem("s_act")
        s_dve = mksem("s_dve")

        em = Emitter(nc)

        guards = {}

        def wait_guard(eng, key):
            g = guards.pop(key, None)
            if g is not None:
                sem, val = g
                em.add(eng, lambda e, sem=sem, val=val: e.wait_ge(sem.h, val))

        def set_guard(key, sem, val=None):
            guards[key] = (sem, sem.val if val is None else val)

        def pe_acquire(b):
            wait_guard("tensor", ("bank", b))

        def bank_release(b, sem):
            set_guard(("bank", b), sem)

        def emit_setup():
            em.add("vector", lambda v: v.memset(c_eps[:, :], EPS_LN))
            em.add("vector", lambda v: v.memset(c_nh[:, :], -0.5))
            v0 = s_dve.inc()
            em.add("vector", lambda v: v.memset(zbd[:, :], 0.0).then_inc(s_dve.h, 1))

            for (dst, src) in ((ident[:, :], ident_in[:, :]),
                               (selones[:, :], selones_in[:, :]),
                               (shiftv[:, :], shift_in[:, :]),
                               (zstage[:, :], z_in[:, :]),
                               (bqstage[:, :], bq_in[:, :])):
                s_wq.inc(16)
                em.add("sync", lambda s, d=dst, i=src: s.dma_start(out=d, in_=i).then_inc(s_wq.h, 16))
            base_wq = s_wq.val

            for h in range(NH):
                wait_guard("sync", "m_f32")
                n = s_wq.inc(16)
                em.add("sync", lambda s, h=h: s.dma_start(out=m_f32[:, :], in_=m_in[h, :, :]).then_inc(s_wq.h, 16))
                em.add("scalar", lambda sc, n=n: sc.wait_ge(s_wq.h, n))
                s_act.inc()
                em.add("scalar", lambda sc, h=h: sc.copy(out=m_bf[:, 128 * h:128 * (h + 1)], in_=m_f32[:, :]).then_inc(s_act.h, 1))
                set_guard("m_f32", s_act)

            em.add("tensor", lambda t, n=base_wq: t.wait_ge(s_wq.h, n))
            pe_acquire(0)
            p1 = s_pe.inc()
            em.add("tensor", lambda t: t.transpose(pb[0][:, 0:8], bqstage[:, :], ident[0:8, 0:8]).then_inc(s_pe.h, 1))
            em.add("scalar", lambda sc, n=p1: sc.wait_ge(s_pe.h, n))
            s_act.inc()
            em.add("scalar", lambda sc: sc.copy(out=bqs[:, :], in_=pb[0][:, 0:8]).then_inc(s_act.h, 1))
            bank_release(0, s_act)
            pe_acquire(1)
            p2 = s_pe.inc()
            em.add("tensor", lambda t: t.transpose(pb[1][:, 0:8], zstage[:, :], ident[0:8, 0:8]).then_inc(s_pe.h, 1))
            em.add("scalar", lambda sc, n=p2: sc.wait_ge(s_pe.h, n))
            a2 = s_act.inc()
            em.add("scalar", lambda sc: sc.copy(out=ztb[:, :], in_=pb[1][:, 0:8]).then_inc(s_act.h, 1))
            bank_release(1, s_act)
            em.add("scalar", lambda sc, n=v0: sc.wait_ge(s_dve.h, n))
            for h in range(NH):
                s_act.inc()
                em.add("scalar", lambda sc, h=h: sc.copy(out=zbd[:, 8 * h + h:8 * h + h + 1], in_=ztb[:, h:h + 1]).then_inc(s_act.h, 1))

            for nb in range(8):
                bi = nb % 2
                wait_guard("sync", ("xt", bi))
                n = s_wq.inc(16)
                em.add("sync", lambda s, nb=nb, bi=bi: s.dma_start(out=xt[bi][:, :], in_=wq_in[128 * nb:128 * (nb + 1), :]).then_inc(s_wq.h, 16))
                em.add("tensor", lambda t, n=n: t.wait_ge(s_wq.h, n))
                for ft in range(FT):
                    b = 2 + (ft % 2)
                    e = ft % 2
                    pe_acquire(b)
                    p = s_pe.inc()
                    em.add("tensor", lambda t, b=b, bi=bi, ft=ft:
                           t.transpose(pb[b][:, 0:128], xt[bi][:, 128 * ft:128 * (ft + 1)], ident[:, :]).then_inc(s_pe.h, 1))
                    em.add("scalar", lambda sc, n=p: sc.wait_ge(s_pe.h, n))
                    wait_guard("scalar", ("evst", e))
                    a = s_act.inc()
                    em.add("scalar", lambda sc, b=b, e=e: sc.copy(out=evst[e][:, :], in_=pb[b][:, 0:128]).then_inc(s_act.h, 1))
                    bank_release(b, s_act)
                    em.add("scalar", lambda sc, a=a: sc.wait_ge(s_act.h, a))
                    s_sp.inc(16)
                    em.add("scalar", lambda sc, ft=ft, nb=nb, e=e:
                           sc.dma_start(out=wqt_scr[ft, :, 128 * nb:128 * (nb + 1)], in_=evst[e][:, :]).then_inc(s_sp.h, 16))
                    set_guard(("evst", e), s_sp)
                set_guard(("xt", bi), s_pe)
            wqt_spilled = s_sp.val

            fold0 = x2t[0]
            for ot in range(FT):
                bi = ot % 2
                wait_guard("sync", ("xt", bi))
                n = s_wq.inc(16)
                em.add("sync", lambda s, ot=ot, bi=bi: s.dma_start(out=xt[bi][:, :], in_=wo_in[128 * ot:128 * (ot + 1), :]).then_inc(s_wq.h, 16))
                em.add("vector", lambda v, n=n: v.wait_ge(s_wq.h, n))
                wait_guard("vector", "fold0")
                dstv = fold0.rearrange("p (q h e) -> p q h e", q=4, h=8)
                srcv = lambda bi: xt[bi].rearrange("p (h g e) -> p h g e", h=8, g=4)
                s_dve.inc()
                em.add("vector", lambda v, bi=bi, dstv=dstv:
                       v.tensor_tensor(out=dstv[:, 1, :, :], in0=srcv(bi)[:, :, 0, :], in1=srcv(bi)[:, :, 1, :], op=ALU.add).then_inc(s_dve.h, 1))
                d2 = s_dve.inc()
                em.add("vector", lambda v, bi=bi, dstv=dstv:
                       v.tensor_tensor(out=dstv[:, 2, :, :], in0=srcv(bi)[:, :, 2, :], in1=srcv(bi)[:, :, 3, :], op=ALU.add).then_inc(s_dve.h, 1))
                em.add("vector", lambda v, n=d2: v.wait_ge(s_dve.h, n))
                d3 = s_dve.inc()
                em.add("vector", lambda v, dstv=dstv:
                       v.tensor_tensor(out=dstv[:, 0, :, :], in0=dstv[:, 1, :, :], in1=dstv[:, 2, :, :], op=ALU.add).then_inc(s_dve.h, 1))
                set_guard(("xt", bi), s_dve)
                em.add("tensor", lambda t, n=d3: t.wait_ge(s_dve.h, n))
                for h in range(NH):
                    b = 2 + (h % 2)
                    e = h % 2
                    pe_acquire(b)
                    p = s_pe.inc()
                    em.add("tensor", lambda t, b=b, h=h: t.transpose(pb[b][:, 0:128], fold0[:, 128 * h:128 * (h + 1)], ident[:, :]).then_inc(s_pe.h, 1))
                    em.add("scalar", lambda sc, n=p: sc.wait_ge(s_pe.h, n))
                    wait_guard("scalar", ("evst", e))
                    a = s_act.inc()
                    em.add("scalar", lambda sc, b=b, e=e: sc.copy(out=evst[e][:, :], in_=pb[b][:, 0:128]).then_inc(s_act.h, 1))
                    bank_release(b, s_act)
                    em.add("scalar", lambda sc, a=a: sc.wait_ge(s_act.h, a))
                    s_sp.inc(16)
                    em.add("scalar", lambda sc, h=h, ot=ot, e=e:
                           sc.dma_start(out=wot_scr[h, :, 128 * ot:128 * (ot + 1)], in_=evst[e][:, :]).then_inc(s_sp.h, 16))
                    set_guard(("evst", e), s_sp)
                set_guard("fold0", s_pe)
            wot_spilled = s_sp.val
            return wqt_spilled, wot_spilled

        def emit_zps():
            jobs = (
                (gz_in, gs_in, zpg_scr, 32, INTER),
                (uz_in, us_in, zpu_scr, 32, INTER),
                (dz_in, ds_in, zpd_scr, NTI, D),
            )
            for (zin, sin, zscr, G, N) in jobs:
                nhalf = 2 if N > 8192 else 1
                for half in range(nhalf):
                    w = N // nhalf
                    c0 = half * w
                    wait_guard("sync", "zsetup")
                    n1 = s_wq.inc(16)
                    em.add("sync", lambda s, zin=zin, G=G, c0=c0, w=w:
                           s.dma_start(out=zsb[0:G, 0:w // 8], in_=zin[:, c0 // 8:(c0 + w) // 8]).then_inc(s_wq.h, 16))
                    n2 = s_wq.inc(16)
                    em.add("sync", lambda s, sin=sin, G=G, c0=c0, w=w:
                           s.dma_start(out=szs16[0:G, 0:w], in_=sin[:, c0:c0 + w]).then_inc(s_wq.h, 16))
                    em.add("vector", lambda v, n=n2: v.wait_ge(s_wq.h, n))
                    for jn in range(8):
                        s_dve.inc()
                        em.add("vector", lambda v, jn=jn, G=G, w=w:
                               v.tensor_scalar(out=zpi[0:G, jn:w:8], in0=zsb[0:G, 0:w // 8],
                                               scalar1=4 * jn, scalar2=15,
                                               op0=ALU.logical_shift_right, op1=ALU.bitwise_and).then_inc(s_dve.h, 1))
                    dv = s_dve.val
                    em.add("vector", lambda v, n=dv: v.wait_ge(s_dve.h, n))
                    d = s_dve.inc()
                    em.add("vector", lambda v, G=G, w=w:
                           v.scalar_tensor_tensor(out=zps16[0:G, 0:w], in0=zpi[0:G, 0:w], scalar=1.0,
                                                  in1=szs16[0:G, 0:w], op0=ALU.add, op1=ALU.mult).then_inc(s_dve.h, 1))
                    em.add("scalar", lambda sc, n=d: sc.wait_ge(s_dve.h, n))
                    s_sp.inc(16)
                    em.add("scalar", lambda sc, zscr=zscr, G=G, c0=c0, w=w:
                           sc.dma_start(out=zscr[:, c0:c0 + w], in_=zps16[0:G, 0:w]).then_inc(s_sp.h, 16))
                    set_guard("zsetup", s_sp)
            guards.pop("zsetup", None)
            return s_sp.val

        dq_state = {"buf": 0}

        def _task_body(qin, row0, sin, srow, zscr, zrow, out_aps, c0, w):
            bi = dq_state["buf"]
            dq_state["buf"] ^= 1
            wait_guard("scalar", "qrep")
            for j in range(8):
                s_rep.inc(16)
                em.add("scalar", lambda sc, j=j, qin=qin, row0=row0, c0=c0, w=w:
                       sc.dma_start(out=qrep[j:128:8, 0:w], in_=qin[row0:row0 + 16, c0:c0 + w]).then_inc(s_rep.h, 16))
            rep_done = s_rep.val
            wait_guard("scalar", ("sbb", bi))
            s_bc.inc(16)
            em.add("scalar", lambda sc, bi=bi, sin=sin, srow=srow, c0=c0, w=w:
                   sc.dma_start(out=sbb[bi][:, 0:w], in_=sin[srow:srow + 1, c0:c0 + w].broadcast_to([128, w])).then_inc(s_bc.h, 16))
            wait_guard("scalar", ("zpbb", bi))
            b2 = s_bc.inc(16)
            em.add("scalar", lambda sc, bi=bi, zscr=zscr, zrow=zrow, c0=c0, w=w:
                   sc.dma_start(out=zpbb[bi][:, 0:w], in_=zscr[zrow:zrow + 1, c0:c0 + w].broadcast_to([128, w])).then_inc(s_bc.h, 16))
            em.add("vector", lambda v, n=rep_done: v.wait_ge(s_rep.h, n))
            wait_guard("vector", "nib")
            d1 = s_dve.inc()
            em.add("vector", lambda v, w=w:
                   v.tensor_scalar(out=nib[:, 0:w], in0=qrep[:, 0:w], scalar1=shiftv[:, 0:1], scalar2=15,
                                   op0=ALU.logical_shift_right, op1=ALU.bitwise_and).then_inc(s_dve.h, 1))
            set_guard("qrep", s_dve)
            em.add("vector", lambda v, n=d1: v.wait_ge(s_dve.h, n))
            em.add("vector", lambda v, n=b2: v.wait_ge(s_bc.h, n))
            wait_guard("vector", "t16")
            d2 = s_dve.inc()
            em.add("vector", lambda v, bi=bi, w=w:
                   v.tensor_tensor(out=t16[:, 0:w], in0=nib[:, 0:w], in1=sbb[bi][:, 0:w], op=ALU.mult).then_inc(s_dve.h, 1))
            set_guard("nib", s_dve)
            set_guard(("sbb", bi), s_dve)
            em.add("vector", lambda v, n=d2: v.wait_ge(s_dve.h, n))
            wait_guard("vector", "wdq")
            d3 = s_dve.inc()
            em.add("vector", lambda v, bi=bi, w=w:
                   v.tensor_tensor(out=wdq[:, 0:w], in0=t16[:, 0:w], in1=zpbb[bi][:, 0:w], op=ALU.subtract).then_inc(s_dve.h, 1))
            set_guard("t16", s_dve)
            set_guard(("zpbb", bi), s_dve)
            em.add("scalar", lambda sc, n=d3: sc.wait_ge(s_dve.h, n))
            for k, oap in enumerate(out_aps):
                ww = min(512 * (k + 1), w) - 512 * k if len(out_aps) > 1 else w
                s_dqw.inc(16)
                em.add("scalar", lambda sc, oap=oap, k=k, ww=ww:
                       sc.dma_start(out=oap, in_=wdq[:, 512 * k:512 * k + ww] if len(out_aps) > 1 else wdq[:, 0:ww]).then_inc(s_dqw.h, 16))
            set_guard("wdq", s_dqw)

        def task_gu(mi, ft, sl):
            qin = (gq_in, uq_in)[mi]
            sin = (gs_in, us_in)[mi]
            zscr = (zpg_scr, zpu_scr)[mi]
            oscr = (g_scr, u_scr)[mi]
            c0 = sum(GU_SLICES[:sl])
            w = GU_SLICES[sl]
            _task_body(qin, 16 * ft, sin, ft, zscr, ft, [oscr[ft, :, c0:c0 + w]], c0, w)

        def task_down(osl, nt):
            c0 = osl * D_OSL
            outs = [d_scr[(c0 + 512 * k) // 512, nt, :, :] for k in range(D_OSL // 512)]
            _task_body(dq_in, 16 * nt, ds_in, nt, zpd_scr, nt, outs, c0, D_OSL)

        dq_tasks = []
        gu_mark = {}
        down_mark = {}
        wc = 0
        for sl in range(6):
            for mi in range(2):
                for ft in range(FT):
                    dq_tasks.append(("gu", mi, ft, sl))
                    wc += 1
                gu_mark[(mi, sl)] = wc * 16
        for osl in range(N_DSL):
            for nt in range(NTI):
                dq_tasks.append(("down", osl, nt))
                wc += 4
            down_mark[osl] = wc * 16
        dq_iter = iter(dq_tasks)
        dq_done = {"done": False, "n": 0}

        def drain_tasks(k):
            if "dq" not in stages:
                dq_done["done"] = True
                return
            if dq_done["done"]:
                return
            for _ in range(k):
                t = next(dq_iter, None)
                if t is None:
                    dq_done["done"] = True
                    return
                dq_done["n"] += 1
                if t[0] == "gu":
                    task_gu(t[1], t[2], t[3])
                else:
                    task_down(t[1], t[2])

        def drain_until(ntasks):
            while not dq_done["done"] and dq_done["n"] < ntasks:
                drain_tasks(1)

        def drain_all():
            while not dq_done["done"]:
                drain_tasks(10)

        def emit_ln_pieces(x_ap, piece_cb):
            for i in range(8):
                if i < 7:
                    em.add("vector", lambda v, i=i: v.bn_stats(out=stats[:, 6 * i:6 * (i + 1)], in_=x_ap[:, 512 * i:512 * (i + 1)]))
                else:
                    s_dve.inc()
                    em.add("vector", lambda v, i=i: v.bn_stats(out=stats[:, 6 * i:6 * (i + 1)], in_=x_ap[:, 512 * i:512 * (i + 1)]).then_inc(s_dve.h, 1))
            d0 = s_dve.val
            em.add("vector", lambda v, n=d0: v.wait_ge(s_dve.h, n))
            d1 = s_dve.inc()
            em.add("vector", lambda v: v.bn_aggr(out=mv[:, :], in_=stats[:, :]).then_inc(s_dve.h, 1))
            em.add("scalar", lambda sc, n=d1: sc.wait_ge(s_dve.h, n))
            a1 = s_act.inc()
            em.add("scalar", lambda sc: sc.activation(out=lnv[:, :], in_=mv[:, 1:2], func=AF.Ln, bias=c_eps[:, 0:1]).then_inc(s_act.h, 1))
            em.add("scalar", lambda sc, n=a1: sc.wait_ge(s_act.h, n))
            a2 = s_act.inc()
            em.add("scalar", lambda sc: sc.activation(out=rstd[:, :], in_=lnv[:, :], func=AF.Exp, scale=c_nh[:, 0:1]).then_inc(s_act.h, 1))
            em.add("vector", lambda v, n=a2: v.wait_ge(s_act.h, n))
            d2 = s_dve.inc()
            em.add("vector", lambda v: v.tensor_scalar(out=mb[:, :], in0=mv[:, 0:1], scalar1=rstd[:, 0:1], scalar2=-1.0,
                                                       op0=ALU.mult, op1=ALU.mult).then_inc(s_dve.h, 1))
            em.add("scalar", lambda sc, n=d2: sc.wait_ge(s_dve.h, n))
            for pi in range(4):
                piece_cb(pi)

        def emit_attn_chunk(c, wqt_spilled, wot_spilled):
            for ts in range(NTS):
                bi = ts % 2
                wait_guard("scalar", ("xt", bi))
                n = s_ld.inc(16)
                em.add("scalar", lambda sc, c=c, ts=ts, bi=bi:
                       sc.dma_start(out=xt[bi][:, :], in_=x_in[512 * c + 128 * ts:512 * c + 128 * (ts + 1), :]).then_inc(s_ld.h, 16))
                em.add("vector", lambda v, n=n: v.wait_ge(s_ld.h, n))
                em.add("scalar", lambda sc, n=n: sc.wait_ge(s_ld.h, n))

                def piece(pi, ts=ts, bi=bi):
                    hb = pi % 2
                    wait_guard("scalar", ("h1b", hb))
                    a = s_act.inc()
                    em.add("scalar", lambda sc, pi=pi, hb=hb, bi=bi:
                           sc.activation(out=h1b[hb][:, :], in_=xt[bi][:, 1024 * pi:1024 * (pi + 1)],
                                         func=AF.Identity, bias=mb[:, 0:1], scale=rstd[:, 0:1]).then_inc(s_act.h, 1))
                    em.add("tensor", lambda t, n=a: t.wait_ge(s_act.h, n))
                    for k in range(8):
                        ft = 8 * pi + k
                        b = 4 + (k % 2)
                        pe_acquire(b)
                        p = s_pe.inc()
                        em.add("tensor", lambda t, b=b, hb=hb, k=k:
                               t.transpose(pb[b][:, 0:128], h1b[hb][:, 128 * k:128 * (k + 1)], ident[:, :]).then_inc(s_pe.h, 1))
                        em.add("scalar", lambda sc, n=p: sc.wait_ge(s_pe.h, n))
                        s_act.inc()
                        em.add("scalar", lambda sc, b=b, ft=ft, ts=ts:
                               sc.copy(out=h1T[ft][:, 128 * ts:128 * (ts + 1)], in_=pb[b][:, 0:128]).then_inc(s_act.h, 1))
                        bank_release(b, s_act)
                    set_guard(("h1b", hb), s_pe)

                emit_ln_pieces(xt[bi], piece)
                set_guard(("xt", bi), s_act)
                if ts == 1:
                    drain_tasks(2)
            h1T_ready = s_act.val

            drain_tasks(1)
            for b in range(8):
                pe_acquire(b)
            em.add("tensor", lambda t, n=h1T_ready: t.wait_ge(s_act.h, n))
            if c == 0:
                em.add("sync", lambda s, n=wqt_spilled: s.wait_ge(s_sp.h, n))
            for ft in range(FT):
                bi = ft % 2
                wait_guard("sync", ("wqtb", bi))
                n = s_wq.inc(16)
                em.add("sync", lambda s, ft=ft, bi=bi: s.dma_start(out=wqtb[bi][:, :], in_=wqt_scr[ft, :, :]).then_inc(s_wq.h, 16))
                em.add("tensor", lambda t, n=n: t.wait_ge(s_wq.h, n))
                for h in range(NH):
                    s_pe.inc()
                    em.add("tensor", lambda t, h=h, ft=ft, bi=bi:
                           t.matmul(pb[h][:, :], wqtb[bi][:, 128 * h:128 * (h + 1)], h1T[ft][:, :],
                                    start=(ft == 0), stop=(ft == FT - 1)).then_inc(s_pe.h, 1))
                set_guard(("wqtb", bi), s_pe)
            qsel_done = s_pe.val

            em.add("vector", lambda v, n=qsel_done: v.wait_ge(s_pe.h, n))
            for h in range(NH):
                hb = h % 2
                if h >= 2:
                    g = guards.pop(("sigadd", hb), None)
                    if g:
                        em.add("scalar", lambda sc, g=g: sc.wait_ge(g[0].h, g[1]))
                s_dve.inc()
                em.add("vector", lambda v, h=h, hb=hb: v.tensor_scalar(out=erex[2 * hb][:, :], in0=pb[h][:, :],
                                                                       scalar1=bqs[:, h:h + 1], scalar2=0.0,
                                                                       op0=ALU.add, op1=ALU.min).then_inc(s_dve.h, 1))
                d2 = s_dve.inc()
                em.add("vector", lambda v, h=h, hb=hb: v.tensor_scalar(out=erex[2 * hb + 1][:, :], in0=pb[h][:, :],
                                                                       scalar1=bqs[:, h:h + 1], scalar2=0.0,
                                                                       op0=ALU.add, op1=ALU.max).then_inc(s_dve.h, 1))
                bank_release(h, s_dve)
                em.add("scalar", lambda sc, n=d2: sc.wait_ge(s_dve.h, n))
                a = s_act.inc()
                em.add("scalar", lambda sc, hb=hb: sc.activation(out=erex[4 + hb][:, :], in_=erex[2 * hb][:, :],
                                                                 func=AF.Exp).then_inc(s_act.h, 1))
                em.add("vector", lambda v, n=a: v.wait_ge(s_act.h, n))
                d3 = s_dve.inc()
                em.add("vector", lambda v, h=h, hb=hb: v.tensor_add(out=sigT[:, 512 * h:512 * (h + 1)],
                                                                    in0=erex[4 + hb][:, :], in1=erex[2 * hb + 1][:, :]).then_inc(s_dve.h, 1))
                set_guard(("sigadd", hb), s_dve)
            sig_done = s_dve.val
            for hb in range(2):
                guards.pop(("sigadd", hb), None)
            drain_tasks(2)

            em.add("tensor", lambda t, n=sig_done: t.wait_ge(s_dve.h, n))
            pe_acquire(2)
            for h in range(NH):
                s_pe.inc()
                em.add("tensor", lambda t, h=h:
                       t.matmul(pb[2][0:8, :], zbd[:, 8 * h:8 * (h + 1)], sigT[:, 512 * h:512 * (h + 1)],
                                start=(h == 0), stop=(h == NH - 1)).then_inc(s_pe.h, 1))
            den_done = s_pe.val
            em.add("vector", lambda v, n=den_done: v.wait_ge(s_pe.h, n))
            d1 = s_dve.inc()
            em.add("vector", lambda v: v.tensor_scalar(out=den_e[:, :], in0=pb[2][0:8, :], scalar1=EPS_DEN, scalar2=None,
                                                       op0=ALU.add).then_inc(s_dve.h, 1))
            bank_release(2, s_dve)
            em.add("vector", lambda v, n=d1: v.wait_ge(s_dve.h, n))
            rden_done = s_dve.inc()
            em.add("vector", lambda v: v.reciprocal(out=rden[:, :], in_=den_e[:, :]).then_inc(s_dve.h, 1))
            for h in range(NH):
                hb = h % 2
                bn = hb
                bb = 3 + hb
                pe_acquire(bn)
                p1 = s_pe.inc()
                em.add("tensor", lambda t, h=h, bn=bn:
                       t.matmul(pb[bn][:, :], m_bf[:, 128 * h:128 * (h + 1)], sigT[:, 512 * h:512 * (h + 1)],
                                start=True, stop=True).then_inc(s_pe.h, 1))
                pe_acquire(bb)
                if h == 0:
                    em.add("tensor", lambda t, n=rden_done: t.wait_ge(s_dve.h, n))
                p2 = s_pe.inc()
                em.add("tensor", lambda t, h=h, bb=bb:
                       t.matmul(pb[bb][:, :], selones[:, 128 * h:128 * (h + 1)], rden[:, :],
                                start=True, stop=True).then_inc(s_pe.h, 1))
                em.add("scalar", lambda sc, n=p2: sc.wait_ge(s_pe.h, n))
                g = guards.pop(("rdb", hb), None)
                if g:
                    em.add("scalar", lambda sc, g=g: sc.wait_ge(g[0].h, g[1]))
                a = s_act.inc()
                em.add("scalar", lambda sc, hb=hb, bb=bb: sc.copy(out=rdb[hb][:, :], in_=pb[bb][:, :]).then_inc(s_act.h, 1))
                bank_release(bb, s_act)
                em.add("vector", lambda v, n=p1: v.wait_ge(s_pe.h, n))
                em.add("vector", lambda v, n=a: v.wait_ge(s_act.h, n))
                s_dve.inc()
                em.add("vector", lambda v, h=h, bn=bn, hb=hb: v.tensor_tensor(out=retT[:, 512 * h:512 * (h + 1)],
                                                                              in0=pb[bn][:, :], in1=rdb[hb][:, :],
                                                                              op=ALU.mult).then_inc(s_dve.h, 1))
                bank_release(bn, s_dve)
                set_guard(("rdb", hb), s_dve)
            ret_done = s_dve.val
            for hb in range(2):
                guards.pop(("rdb", hb), None)
            drain_tasks(2)

            em.add("tensor", lambda t, n=ret_done: t.wait_ge(s_dve.h, n))
            if c == 0:
                em.add("sync", lambda s, n=wot_spilled: s.wait_ge(s_sp.h, n))
            for ts in range(NTS):
                bi = ts % 2
                x2i = ts % 2
                wait_guard("scalar", ("xt", bi))
                em.add("scalar", lambda sc, n=ret_done: sc.wait_ge(s_dve.h, n))
                nx = s_ld.inc(16)
                em.add("scalar", lambda sc, c=c, ts=ts, bi=bi:
                       sc.dma_start(out=xt[bi][:, :], in_=x_in[512 * c + 128 * ts:512 * c + 128 * (ts + 1), :]).then_inc(s_ld.h, 16))
                for b in range(8):
                    pe_acquire(b)
                for he in range(NH):
                    wi = he % 2
                    wait_guard("sync", ("wotb", wi))
                    nw = s_wq.inc(16)
                    em.add("sync", lambda s, he=he, wi=wi: s.dma_start(out=wotb[wi][:, :], in_=wot_scr[he, :, :]).then_inc(s_wq.h, 16))
                    em.add("tensor", lambda t, n=nw: t.wait_ge(s_wq.h, n))
                    for oc in range(8):
                        s_pe.inc()
                        em.add("tensor", lambda t, he=he, oc=oc, ts=ts, wi=wi:
                               t.matmul(pb[oc][:, :], retT[:, 512 * he + 128 * ts:512 * he + 128 * (ts + 1)],
                                        wotb[wi][:, 512 * oc:512 * (oc + 1)],
                                        start=(he == 0), stop=(he == NH - 1)).then_inc(s_pe.h, 1))
                    set_guard(("wotb", wi), s_pe)
                ao_done = s_pe.val
                wait_guard("vector", ("x2t", x2i))
                em.add("vector", lambda v, n=ao_done: v.wait_ge(s_pe.h, n))
                em.add("vector", lambda v, n=nx: v.wait_ge(s_ld.h, n))
                for oc in range(8):
                    s_dve.inc()
                    em.add("vector", lambda v, oc=oc, bi=bi, x2i=x2i:
                           v.tensor_tensor(out=x2t[x2i][:, 512 * oc:512 * (oc + 1)], in0=pb[oc][:, :],
                                           in1=xt[bi][:, 512 * oc:512 * (oc + 1)], op=ALU.add).then_inc(s_dve.h, 1))
                    bank_release(oc, s_dve)
                x2_done = s_dve.val
                set_guard(("xt", bi), s_dve)
                em.add("scalar", lambda sc, n=x2_done: sc.wait_ge(s_dve.h, n))
                s_sp.inc(16)
                em.add("scalar", lambda sc, c=c, ts=ts, x2i=x2i:
                       sc.dma_start(out=x2_scr[c, 128 * ts:128 * (ts + 1), :], in_=x2t[x2i][:, :]).then_inc(s_sp.h, 16))

                def piece2(pi, ts=ts, x2i=x2i):
                    hb = pi % 2
                    wait_guard("scalar", ("h2b", hb))
                    a = s_act.inc()
                    em.add("scalar", lambda sc, pi=pi, hb=hb, x2i=x2i:
                           sc.activation(out=h2b[hb][:, :], in_=x2t[x2i][:, 1024 * pi:1024 * (pi + 1)],
                                         func=AF.Identity, bias=mb[:, 0:1], scale=rstd[:, 0:1]).then_inc(s_act.h, 1))
                    em.add("scalar", lambda sc, a=a: sc.wait_ge(s_act.h, a))
                    s_sp.inc(16)
                    em.add("scalar", lambda sc, c=c, ts=ts, pi=pi, hb=hb:
                           sc.dma_start(out=h2_scr[c, 128 * ts:128 * (ts + 1), 1024 * pi:1024 * (pi + 1)],
                                        in_=h2b[hb][:, :]).then_inc(s_sp.h, 16))
                    set_guard(("h2b", hb), s_sp)

                emit_ln_pieces(x2t[x2i], piece2)
                set_guard(("x2t", x2i), s_act)
                drain_tasks(1)

        def emit_ffn_chunk(c, h2sp_mark):
            em.add("sync", lambda s, n=h2sp_mark: s.wait_ge(s_sp.h, n))
            em.add("scalar", lambda sc, n=h2sp_mark: sc.wait_ge(s_sp.h, n))
            if c > 0:
                wait_guard("sync", "h2T")
            for ft in range(FT):
                s_wq.inc(16)
                em.add("sync", lambda s, c=c, ft=ft:
                       s.dma_start(out=h2T[ft][:, :], in_=h2_scr[c, :, 128 * ft:128 * (ft + 1)], transpose=True).then_inc(s_wq.h, 16))
            h2t_done = s_wq.val

            nc0 = 0
            for blk, nblk in enumerate(GU_BLOCKS):
                sl = blk // 2
                for mi, (scr, mark) in enumerate(((g_scr, gu_mark[(0, sl)]), (u_scr, gu_mark[(1, sl)]))):
                    if c == 0:
                        drain_until(64 * sl + 32 * (mi + 1))
                    em.add("sync", lambda s, n=mark: s.wait_ge(s_dqw.h, n))
                    for b in range(nblk):
                        pe_acquire(b)
                    if blk == 0 and mi == 0:
                        em.add("tensor", lambda t, n=h2t_done: t.wait_ge(s_wq.h, n))
                    for ft in range(FT):
                        gi = ft % 3
                        wait_guard("sync", ("gub", gi))
                        n = s_wq.inc(16)
                        em.add("sync", lambda s, scr=scr, ft=ft, nc0=nc0, nblk=nblk, gi=gi:
                               s.dma_start(out=gub[gi][:, 0:128 * nblk], in_=scr[ft, :, 128 * nc0:128 * (nc0 + nblk)]).then_inc(s_wq.h, 16))
                        em.add("tensor", lambda t, n=n: t.wait_ge(s_wq.h, n))
                        for b in range(nblk):
                            s_pe.inc()
                            em.add("tensor", lambda t, b=b, ft=ft, gi=gi:
                                   t.matmul(pb[b][:, :], gub[gi][:, 128 * b:128 * (b + 1)], h2T[ft][:, :],
                                            start=(ft == 0), stop=(ft == FT - 1)).then_inc(s_pe.h, 1))
                        set_guard(("gub", gi), s_pe)
                    chain_done = s_pe.val
                    if mi == 0:
                        em.add("scalar", lambda sc, n=chain_done: sc.wait_ge(s_pe.h, n))
                        for b in range(nblk):
                            s_act.inc()
                            em.add("scalar", lambda sc, b=b: sc.activation(out=silu_st[b][:, :], in_=pb[b][:, :],
                                                                           func=AF.Silu).then_inc(s_act.h, 1))
                            bank_release(b, s_act)
                    else:
                        em.add("vector", lambda v, n=chain_done: v.wait_ge(s_pe.h, n))
                        av = s_act.val
                        em.add("vector", lambda v, n=av: v.wait_ge(s_act.h, n))
                        for b in range(nblk):
                            s_dve.inc()
                            em.add("vector", lambda v, b=b, nt=nc0 + b:
                                   v.tensor_tensor(out=inter[nt][:, :], in0=pb[b][:, :], in1=silu_st[b][:, :],
                                                   op=ALU.mult).then_inc(s_dve.h, 1))
                            bank_release(b, s_dve)
                if c == 0:
                    drain_tasks(2)
                nc0 += nblk
            inter_done = s_dve.val
            set_guard("h2T", s_pe)

            em.add("tensor", lambda t, n=inter_done: t.wait_ge(s_dve.h, n))
            for oc in range(NOC):
                if c == 0:
                    drain_until(384 + NTI * (oc // (D_OSL // 512) + 1))
                em.add("sync", lambda s, n=down_mark[oc // (D_OSL // 512)]: s.wait_ge(s_dqw.h, n))
                banks = [4 * (oc % 2) + i for i in range(4)]
                for b in banks:
                    pe_acquire(b)
                for nt in range(NTI):
                    wi = nt % 2
                    wait_guard("sync", ("wdb", wi))
                    n = s_wq.inc(16)
                    em.add("sync", lambda s, oc=oc, nt=nt, wi=wi:
                           s.dma_start(out=wdb[wi][:, :], in_=d_scr[oc, nt, :, :]).then_inc(s_wq.h, 16))
                    em.add("tensor", lambda t, n=n: t.wait_ge(s_wq.h, n))
                    for ts in range(NTS):
                        s_pe.inc()
                        em.add("tensor", lambda t, b=banks[ts], nt=nt, ts=ts, wi=wi:
                               t.matmul(pb[b][:, :], inter[nt][:, 128 * ts:128 * (ts + 1)], wdb[wi][:, :],
                                        start=(nt == 0), stop=(nt == NTI - 1)).then_inc(s_pe.h, 1))
                    set_guard(("wdb", wi), s_pe)
                down_done = s_pe.val
                em.add("scalar", lambda sc, n=down_done: sc.wait_ge(s_pe.h, n))
                for ts in range(NTS):
                    fi = ts % 2
                    wait_guard("scalar", ("x2cb", fi))
                    nx = s_ld.inc(16)
                    em.add("scalar", lambda sc, c=c, ts=ts, oc=oc, fi=fi:
                           sc.dma_start(out=x2cb[fi][:, :], in_=x2_scr[c, 128 * ts:128 * (ts + 1), 512 * oc:512 * (oc + 1)]).then_inc(s_ld.h, 16))
                    wait_guard("scalar", ("f16b", fi))
                    a = s_act.inc()
                    em.add("scalar", lambda sc, b=banks[ts], fi=fi: sc.activation(out=f16b[fi][:, :], in_=pb[b][:, :],
                                                                                  func=AF.Copy).then_inc(s_act.h, 1))
                    bank_release(banks[ts], s_act)
                    em.add("vector", lambda v, n=a: v.wait_ge(s_act.h, n))
                    em.add("vector", lambda v, n=nx: v.wait_ge(s_ld.h, n))
                    wait_guard("vector", ("outcb", fi))
                    d = s_dve.inc()
                    em.add("vector", lambda v, fi=fi: v.tensor_tensor(out=outcb[fi][:, :], in0=f16b[fi][:, :],
                                                                      in1=x2cb[fi][:, :], op=ALU.add).then_inc(s_dve.h, 1))
                    set_guard(("f16b", fi), s_dve)
                    set_guard(("x2cb", fi), s_dve)
                    em.add("scalar", lambda sc, n=d: sc.wait_ge(s_dve.h, n))
                    s_out.inc(16)
                    em.add("scalar", lambda sc, c=c, ts=ts, oc=oc, fi=fi:
                           sc.dma_start(out=y_out[512 * c + 128 * ts:512 * c + 128 * (ts + 1), 512 * oc:512 * (oc + 1)],
                                        in_=outcb[fi][:, :]).then_inc(s_out.h, 16))
                    set_guard(("outcb", fi), s_out)
                if c == 0:
                    drain_tasks(2)

        wqt_spilled = wot_spilled = 0
        if "setup" in stages:
            wqt_spilled, wot_spilled = emit_setup()
        if "zps" in stages:
            zps_done = emit_zps()
            em.add("scalar", lambda sc, n=zps_done: sc.wait_ge(s_sp.h, n))

        h2sp_marks = {}
        if "attn" in stages:
            for c in range(nch):
                emit_attn_chunk(c, wqt_spilled, wot_spilled)
                h2sp_marks[c] = s_sp.val
        if "ffn" in stages:
            for c in range(nch):
                if c == 1:
                    drain_all()
                emit_ffn_chunk(c, h2sp_marks[c])
        drain_all()

        if s_out.val:
            em.add("gpsimd", lambda g, n=s_out.val: g.wait_ge(s_out.h, n))
            em.add("sync", lambda s, n=s_out.val: s.wait_ge(s_out.h, n))
        else:
            em.add("gpsimd", lambda g, n=s_sp.val: g.wait_ge(s_sp.h, n))
            em.add("gpsimd", lambda g, n=s_dqw.val: g.wait_ge(s_dqw.h, n))
        if dbg:
            scrmap = dict(wqt=wqt_scr, wot=wot_scr, g=g_scr, u=u_scr, d=d_scr,
                          zpg=zpg_scr, zpu=zpu_scr, zpd=zpd_scr, h2=h2_scr, x2=x2_scr)
            for name, (oap, key) in dbg_outs.items():
                src_ap = scrmap[key][tuple(slice(None) for _ in scrmap[key].shape)]
                s_out.inc(16)
                em.add("gpsimd", lambda g, oap=oap, src_ap=src_ap:
                       g.dma_start(out=oap, in_=src_ap).then_inc(s_out.h, 16))
            em.add("gpsimd", lambda g, n=s_out.val: g.wait_ge(s_out.h, n))
        em.flush()

    return nc


def _shard_inputs(inputs):
    x = np.ascontiguousarray(np.asarray(inputs["L0_attn_hidden_states"]).reshape(16384, D))
    sel = (np.arange(NH)[:, None] * 512 + np.arange(128)[None, :]).ravel()
    wq = np.ascontiguousarray(np.asarray(inputs["L0_attn_Wq"])[sel])
    bq = np.ascontiguousarray(np.asarray(inputs["L0_attn_bq"])[sel].reshape(NH, 128))
    wo = np.ascontiguousarray(np.asarray(inputs["L0_attn_Wo"]))
    m = np.ascontiguousarray(np.asarray(inputs["L0_attn_M"]))
    z = np.ascontiguousarray(np.asarray(inputs["L0_attn_z"]))
    ident = np.eye(128, dtype=np.float32)
    selones = np.zeros((NH, NH * 128), np.float32)
    for h in range(NH):
        selones[h, 128 * h:128 * (h + 1)] = 1.0
    shiftv = (4 * (np.arange(128) % 8)).astype(np.int32)[:, None]

    common = dict(
        wq=wq, bq=bq, wo=wo, m=m, z=z,
        gq=np.ascontiguousarray(np.asarray(inputs["L0_ffn_gate_qweight"])),
        gs=np.ascontiguousarray(np.asarray(inputs["L0_ffn_gate_scales"])),
        gz=np.ascontiguousarray(np.asarray(inputs["L0_ffn_gate_zeros"])),
        uq=np.ascontiguousarray(np.asarray(inputs["L0_ffn_up_qweight"])),
        us=np.ascontiguousarray(np.asarray(inputs["L0_ffn_up_scales"])),
        uz=np.ascontiguousarray(np.asarray(inputs["L0_ffn_up_zeros"])),
        dq=np.ascontiguousarray(np.asarray(inputs["L0_ffn_down_qweight"])),
        ds=np.ascontiguousarray(np.asarray(inputs["L0_ffn_down_scales"])),
        dz=np.ascontiguousarray(np.asarray(inputs["L0_ffn_down_zeros"])),
        ident=ident, selones=selones, shiftv=shiftv,
    )
    in_maps = []
    for cidx in range(NCORES):
        im = dict(common)
        im["x"] = np.ascontiguousarray(x[TCORE * cidx: TCORE * (cidx + 1)])
        in_maps.append(im)
    return in_maps


def kernel(**inputs):
    in_maps = _shard_inputs(inputs)
    nc = build_program()
    res = run_bass_kernel_spmd(nc, in_maps, core_ids=list(range(NCORES)))
    out = np.concatenate([res.results[c]["y"] for c in range(NCORES)], axis=0)
    return out.reshape(4, 4096, D)


if __name__ == "__main__":
    nc = build_program()
    n = sum(1 for _ in nc.all_instructions())
    print("instructions:", n)
